# version 17
# baseline (speedup 1.0000x reference)
import numpy as np
from contextlib import ExitStack

import concourse.bass as bass
import concourse.mybir as mybir
from concourse.bass_utils import run_bass_kernel_spmd

F32 = mybir.dt.float32
BF16 = mybir.dt.bfloat16
AF = mybir.ActivationFunctionType
ALU = mybir.AluOpType
AX = mybir.AxisListType

T = 16
C4 = 4
NCH = 64
HQ = 4
SPP = 8

ENGS = ("tensor", "vector", "scalar", "gpsimd", "sync")

OFF_S = 16640
OFF_SLOT0 = 32736
OFF_SLOT1 = 98272
OFF_SLOT2 = 163808


class Plan:
    def __init__(self):
        self.q = {e: [] for e in ENGS}
        self.cnt = {}
        self._lastwait = {e: {} for e in ENGS}

    def emit(self, eng, fn, waits=(), inc=None, tag=""):
        ws = []
        for s, v in waits:
            if v is None or v <= 0:
                continue
            self.cnt.setdefault(s, 0)
            if self._lastwait[eng].get(s, 0) >= v:
                continue
            self._lastwait[eng][s] = v
            ws.append((s, v))
        after = None
        if inc is not None:
            s, n = inc
            self.cnt[s] = self.cnt.get(s, 0) + n
            after = self.cnt[s]
        self.q[eng].append((fn, tuple(ws), inc, tag))
        return after

    def verify(self):
        ptr = {e: 0 for e in ENGS}
        cnt = {s: 0 for s in self.cnt}
        progress = True
        while progress:
            progress = False
            for e in ENGS:
                while ptr[e] < len(self.q[e]):
                    fn, ws, inc, tag = self.q[e][ptr[e]]
                    if any(cnt[s] < v for s, v in ws):
                        break
                    if inc is not None:
                        cnt[inc[0]] += inc[1]
                    ptr[e] += 1
                    progress = True
        stuck = {e: (ptr[e], len(self.q[e])) for e in ENGS if ptr[e] < len(self.q[e])}
        if stuck:
            lines = []
            for e, (p, n) in stuck.items():
                fn, ws, inc, tag = self.q[e][p]
                missing = [(s, v, cnt.get(s, 0)) for s, v in ws if cnt.get(s, 0) < v]
                lines.append(f"{e} stuck at {p}/{n} tag={tag} missing={missing}")
            raise RuntimeError("plan deadlock:\n" + "\n".join(lines))


def build_nc(debug=False):
    nc = bass.Bass("TRN2", target_bir_lowering=False)

    d_x = nc.dram_tensor("x", [2048, 2048], F32, kind="ExternalInput").ap()
    d_wq = nc.dram_tensor("wq_blk", [T, 128, T, 128], F32, kind="ExternalInput").ap()
    d_wk = nc.dram_tensor("wk", [2048, 2048], F32, kind="ExternalInput").ap()
    d_wv = nc.dram_tensor("wv_blk", [T, 128, T, 128], F32, kind="ExternalInput").ap()
    d_wo = nc.dram_tensor("wo", [2048, 2048], F32, kind="ExternalInput").ap()
    d_w0 = nc.dram_tensor("w0t_blk", [NCH, 128, T, 128], F32, kind="ExternalInput").ap()
    d_w1 = nc.dram_tensor("w1t", [8192, 2048], F32, kind="ExternalInput").ap()
    d_b0 = nc.dram_tensor("bias0_t", [128, NCH], F32, kind="ExternalInput").ap()
    d_b1 = nc.dram_tensor("bias1_rep", [128, 2048], F32, kind="ExternalInput").ap()
    d_idn = nc.dram_tensor("iden", [128, 128], F32, kind="ExternalInput").ap()
    d_out = nc.dram_tensor("out", [2048, 2048], F32, kind="ExternalOutput").ap()

    skind = "ExternalOutput" if debug else "Internal"
    d_xnt = nc.dram_tensor("xnt_hbm", [T, 128, 2048], BF16, kind=skind).ap()
    d_G = nc.dram_tensor("G_hbm", [SPP, 128, T, 256], BF16, kind=skind).ap()
    d_A = nc.dram_tensor("A_hbm", [T, 128, 2048], BF16, kind=skind).ap()
    d_dt = nc.dram_tensor("dt_hbm", [SPP, 128, NCH, 256], BF16, kind=skind).ap()
    d_g = nc.dram_tensor("g_hbm", [T, 128, 2048], BF16, kind=skind).ap()
    dbg = {}
    if debug:
        for nm in ("t1_dbg", "b_dbg", "vt_dbg", "xn_dbg", "xn2_dbg", "xnt1_dbg"):
            dbg[nm] = nc.dram_tensor(nm, [128, T * 2048], BF16, kind="ExternalOutput").ap()

    def sb(name, off, shape, dtype):
        return nc.alloc_sbuf_tensor_at(name, shape, dtype, offset=off)

    xb = [sb(f"xb{i}", OFF_SLOT1 + i * 8192, [128, 2048], F32) for i in range(4)]
    sqj = sb("sqj", OFF_S, [128, 2048], F32)
    STT = OFF_S + 14336
    rsu = [sb(f"rsu{i}", STT + i * 192, [128, 1], F32) for i in range(2)]
    nmean = [sb(f"nmean{i}", STT + 32 + i * 192, [128, 1], F32) for i in range(2)]
    ssq = [sb(f"ssq{i}", STT + 64 + i * 192, [128, 1], F32) for i in range(2)]
    stdv = [sb(f"stdv{i}", STT + 96 + i * 192, [128, 1], F32) for i in range(2)]
    rstd = [sb(f"rstd{i}", STT + 128 + i * 192, [128, 1], F32) for i in range(2)]
    eps_sb = sb("eps_sb", STT + 160, [128, 1], F32)
    idn_sb = sb("idn_sb", STT + 384, [128, 128], BF16)
    mst = [sb("mst0", STT + 896, [128, 128], BF16),
           sb("mst1", STT + 1152, [128, 128], BF16)]
    xn_sb = sb("xn_sb", OFF_SLOT0, [128, T * 2048], BF16)
    b0_sb = sb("b0_sb", STT + 640, [128, NCH], F32)
    wsb = [sb(f"wsb{i}", OFF_S + i * 4096, [128, 2048], BF16) for i in range(3)]
    Gst = [sb("Gst0", OFF_S + 8192, [128, 2048], BF16),
           sb("Gst1", OFF_SLOT1 + 61440, [128, 2048], BF16)]
    xnt1 = sb("xnt1", OFF_SLOT2, [128, T * 2048], BF16)
    w0b = [sb("w0b0", OFF_S, [128, 2048], BF16),
           sb("w0b1", OFF_S + 4096, [128, 2048], BF16)]
    dt_st = [sb("dtst0", OFF_S + 8192, [128, 2048], BF16),
             sb("dtst1", OFF_SLOT1 + 61440, [128, 2048], BF16)]
    w1q = [sb("w1q_a", OFF_SLOT1, [128, NCH * 512], BF16),
           sb("w1q_b", OFF_SLOT0, [128, NCH * 512], BF16)]
    dts = [sb("dts0", OFF_SLOT2, [128, NCH * 256], BF16),
           sb("dts1", OFF_SLOT2 + 32768, [128, NCH * 256], BF16)]
    b1_sb = sb("b1_sb", OFF_S, [128, 2048], F32)
    e_st = [sb("est0", OFF_S + 10240, [128, 512], F32),
            sb("est1", OFF_S + 12288, [128, 512], F32)]
    gst = [sb("gst0", OFF_S + 8192, [128, 512], BF16),
           sb("gst1", OFF_S + 9216, [128, 512], BF16)]
    wk_sb = sb("wk_sb", OFF_SLOT1, [128, T * 2048], BF16)
    gpr = [sb("gpr0", OFF_SLOT2, [128, T * 256], BF16),
           sb("gpr1", OFF_SLOT2 + 8192, [128, T * 256], BF16)]
    t1_sb = sb("t1_sb", OFF_SLOT0, [128, T * 2048], BF16)
    expb = sb("expb", OFF_SLOT2, [128, 2048], F32)
    Ast = [sb("Ast0", OFF_SLOT2 + 8192, [128, 2048], BF16),
           sb("Ast1", OFF_SLOT2 + 12288, [128, 2048], BF16)]
    smx = sb("smx", STT, [128, 1], F32)
    snmx = sb("snmx", STT + 32, [128, 1], F32)
    srsm = sb("srsm", STT + 64, [128, 1], F32)
    srrc = sb("srrc", STT + 96, [128, 1], F32)
    xnt2 = [sb(f"xnt2_{e}", OFF_SLOT1 + e * 4096, [128, 2048], BF16) for e in range(T)]
    vt_sb = sb("vt_sb", OFF_SLOT2, [128, T * 2048], BF16)
    wo_sb = sb("wo_sb", OFF_SLOT0, [128, T * 2048], BF16)
    b_sb = sb("b_sb", OFF_SLOT1, [128, T * 2048], BF16)
    xb2 = [sb("xb2_0", OFF_SLOT0, [128, 2048], F32),
           sb("xb2_1", OFF_SLOT0 + 8192, [128, 2048], F32)]
    gb = [sb("gb0", OFF_SLOT0 + 16384, [128, 2048], BF16),
          sb("gb1", OFF_SLOT0 + 20480, [128, 2048], BF16)]
    tmpb = sb("tmpb", OFF_SLOT0 + 24576, [128, 2048], F32)
    ob = [sb("ob0", OFF_SLOT0 + 32768, [128, 2048], F32),
          sb("ob1", OFF_SLOT0 + 40960, [128, 2048], F32)]

    ps = [nc.alloc_psum_tensor("pA", [128, 2048], F32),
          nc.alloc_psum_tensor("pB", [128, 2048], F32)]

    P = Plan()
    pfree = [("v", 0), ("v", 0)]
    pready = [0, 0]

    def MM(eng_unused, out, lhsT, rhs, start, stop):
        return lambda e, o=out, l=lhsT, r=rhs, st=start, sp=stop: e.matmul(
            o, l, r, start=st, stop=sp)

    v_eps = P.emit("vector", lambda e, o=eps_sb[:]: e.memset(o, 1e-5),
                   [], ("v", 1), "eps")
    P.emit("gpsimd", lambda e, o=idn_sb[:], i=d_idn: e.dma_start(out=o, in_=i),
           [], ("idn", 16), "idnload")
    P.emit("gpsimd", lambda e, o=b0_sb[:], i=d_b0: e.dma_start(out=o, in_=i),
           [], ("b0", 16), "b0load")

    vA = [0] * T
    sB = [0] * T
    vC = [0] * T
    tT = [0] * T
    vE = [0] * T

    def emit_x(st_):
        px = st_ % 4
        waits = []
        if st_ >= 4:
            waits = [("s", sB[st_ - 4]), ("v", vC[st_ - 4])]
        P.emit("sync", lambda e, o=xb[px][:], i=d_x[st_ * 128:(st_ + 1) * 128, :]:
               e.dma_start(out=o, in_=i), waits, (f"x{px}", 16), f"xload{st_}")

    def emit_A(st_):
        p = st_ % 2
        px = st_ % 4
        xw = (f"x{px}", 16 * (st_ // 4 + 1))
        w = [xw]
        if st_ >= 2:
            w.append(("s", sB[st_ - 2]))
        P.emit("vector", lambda e, o=rsu[p][:], i=xb[px][:]:
               e.reduce_sum(o, i, axis=AX.X), w, ("v", 1), f"lnsum{st_}")
        vA[st_] = P.emit("vector", lambda e, o=nmean[p][:], i=rsu[p][:]:
                         e.tensor_scalar(o, i, -1.0 / 2048.0, None, op0=ALU.mult),
                         [("v", P.cnt["v"])], ("v", 1), f"lnm{st_}")

    def emit_B(st_):
        p = st_ % 2
        px = st_ % 4
        w = [("v", vA[st_])]
        if st_ >= 2:
            w.append(("v", vC[st_ - 2]))
        c3 = P.emit("scalar", lambda e, o=sqj[:], i=xb[px][:], b=nmean[p][:], a=ssq[p][:]:
                    e.activation(o, i, AF.Square, bias=b, accum_out=a),
                    w, ("s", 1), f"lnsq{st_}")
        sB[st_] = P.emit("scalar", lambda e, o=stdv[p][:], i=ssq[p][:], b=eps_sb[:]:
                         e.activation(o, i, AF.Sqrt, bias=b, scale=1.0 / 2048.0),
                         [("s", c3), ("v", v_eps)], ("s", 1), f"lnsd{st_}")

    def emit_C(st_):
        p = st_ % 2
        px = st_ % 4
        c5 = P.emit("vector", lambda e, o=rstd[p][:], i=stdv[p][:]:
                    e.reciprocal(o, i), [("s", sB[st_])], ("v", 1), f"lnrs{st_}")
        vC[st_] = P.emit("vector",
                         lambda e, o=xn_sb[:, st_ * 2048:(st_ + 1) * 2048], i=xb[px][:],
                         m=nmean[p][:], r=rstd[p][:]:
                         e.tensor_scalar(o, i, m, r, op0=ALU.add, op1=ALU.mult),
                         [("v", c5)], ("v", 1), f"lnxn{st_}")

    def emit_T(st_):
        p = st_ % 2
        first = [("v", vC[st_]), ("idn", 16), pfree[p]]
        for et in range(T):
            inc = ("t", 1) if et == T - 1 else None
            P.emit("tensor", MM(None, ps[p][:, et * 128:(et + 1) * 128],
                                xn_sb[:, st_ * 2048 + et * 128: st_ * 2048 + (et + 1) * 128],
                                idn_sb[:], et % 4 == 0, et % 4 == 3),
                   first if et == 0 else [], inc, f"TP{st_}")
        tT[st_] = P.cnt["t"]
        pready[p] = tT[st_]

    def emit_E(st_):
        p = st_ % 2
        vE[st_] = P.emit(
            "scalar",
            lambda e,
            o=xnt1[:].rearrange("p (et s2) -> p et s2", et=T)[:, :, st_ * 128:(st_ + 1) * 128],
            i=ps[p][:].rearrange("p (et s) -> p et s", et=T):
            e.copy(o, i), [("t", tT[st_])], ("s", 1), f"TPev{st_}")
        pfree[p] = ("s", vE[st_])

    for i in range(4):
        emit_x(i)
    emit_A(0)
    emit_B(0)
    emit_A(1)
    emit_B(1)
    for st_ in range(T):
        emit_C(st_)
        emit_T(st_)
        emit_E(st_)
        if st_ + 4 < T:
            emit_x(st_ + 4)
        if st_ + 2 < T:
            emit_A(st_ + 2)
            emit_B(st_ + 2)
    v_ln_last = vC[T - 1]
    s_sq_last = sB[T - 1]
    vE_last = vE[T - 1]
    if debug:
        P.emit("sync", lambda e, o=dbg["xn_dbg"], i=xn_sb[:]: e.dma_start(out=o, in_=i),
               [("v", v_ln_last)], ("dbgx", 16), "dbg_xn")

    for e_ in range(T):
        P.emit("scalar", lambda e, o=d_xnt[e_], i=xnt1[:, e_ * 2048:(e_ + 1) * 2048]:
               e.dma_start(out=o, in_=i),
               [("s", vE_last)], ("xtw", 16), f"xtspill{e_}")
    xtw_all = P.cnt["xtw"]

    w1q_ready = [0] * HQ

    def load_w1q_seg(q, n0, n1, waits):
        buf = w1q[0] if q % 2 == 0 else w1q[1]
        step = 8
        for s0 in range(n0, n1, step):
            s1 = min(s0 + step, n1)
            P.emit("gpsimd",
                   lambda e, o=buf[:].rearrange("p (n h) -> p n h", n=NCH)[:, s0:s1, :],
                   i=d_w1.rearrange("(n p) h -> p n h", p=128)[:, s0:s1,
                                                              q * 512:(q + 1) * 512]:
                   e.dma_start(out=o, in_=i),
                   waits if s0 == n0 else [], (f"w1q{q}", 16), f"w1l{q}.{s0}")
        w1q_ready[q] = P.cnt[f"w1q{q}"]

    def load_w1q(q, waits):
        load_w1q_seg(q, 0, NCH, waits)

    load_w1q_seg(0, 32, 60, [])

    t_L1 = [0] * NCH
    s_gelu = [0] * NCH
    for nch in range(NCH):
        p = nch % 2
        w = [("s", s_sq_last)] if nch < 2 else [("t", t_L1[nch - 2])]
        P.emit("gpsimd", lambda e, o=w0b[p][:].rearrange("p (c m) -> p c m", c=T),
               i=d_w0[nch]: e.dma_start(out=o, in_=i),
               w, (f"w0_{p}", 16), f"w0l{nch}")
        w0w = (f"w0_{p}", P.cnt[f"w0_{p}"])
        first = [w0w, ("s", vE_last), pfree[p]]
        for c in range(C4):
            for ec in range(T):
                stop = (ec == T - 1)
                inc = ("t", 1) if (stop and c == C4 - 1) else None
                P.emit("tensor", MM(None, ps[p][:, c * 512:(c + 1) * 512],
                                    w0b[p][:, ec * 128:(ec + 1) * 128],
                                    xnt1[:, ec * 2048 + c * 512: ec * 2048 + (c + 1) * 512],
                                    ec == 0, stop),
                       first if (ec == 0 and c == 0) else [], inc, f"L1_{nch}")
        tg = P.cnt["t"]
        t_L1[nch] = tg
        pready[p] = tg
        w = [("t", tg), ("b0", 16)]
        if debug:
            w.append(("dbgx", 16))
        if nch >= 2:
            w.append((f"dtw{p}", 16 * (nch // 2)))
        cs = P.emit("scalar", lambda e, o=dt_st[p][:], i=ps[p][:], b=b0_sb[:, nch:nch + 1]:
                    e.activation(o, i, AF.Gelu, bias=b),
                    w, ("s", 1), f"gelu{nch}")
        s_gelu[nch] = cs
        pfree[p] = ("s", cs)
        P.emit("scalar", lambda e, o=d_dt[:, :, nch, :].rearrange("sp p s -> p sp s"),
               i=dt_st[p][:].rearrange("p (sp s) -> p sp s", sp=SPP):
               e.dma_start(out=o, in_=i), [("s", cs)], (f"dtw{p}", 16), f"dtsp{nch}")
        if nch in (2, 4, 6, 8):
            i0 = 8 * (nch // 2 - 1)
            load_w1q_seg(0, i0, i0 + 8,
                         [("s", s_sq_last), ("v", v_ln_last)] if nch == 2 else [])
    t_L1_last = t_L1[NCH - 1]
    w1q0_partial = P.cnt["w1q0"]
    dtw_all = [("dtw0", P.cnt["dtw0"]), ("dtw1", P.cnt["dtw1"])]
    if debug:
        P.emit("sync", lambda e, o=dbg["xn2_dbg"], i=xn_sb[:]: e.dma_start(out=o, in_=i),
               [("t", t_L1_last)], ("dbg2", 16), "dbg_xn2")
        P.emit("sync", lambda e, o=dbg["xnt1_dbg"], i=xnt1[:]: e.dma_start(out=o, in_=i),
               [("t", t_L1_last)], ("dbg2", 16), "dbg_xnt1")

    dtr_cnt0 = [0, 0]
    dtr_pre = {}
    for sp in range(2):
        p = sp % 2
        w = dtw_all + [("t", t_L1_last), ("xtw", xtw_all)]
        for h in range(2):
            sem = f"dtrh{sp}" if h == 0 else f"dtr{p}"
            if h == 1:
                dtr_cnt0[p] += 16
            P.emit("sync",
                   lambda e, o=dts[p][:].rearrange("p (n s) -> p n s", n=NCH)[:, h * 32:(h + 1) * 32, :],
                   i=d_dt[sp][:, h * 32:(h + 1) * 32, :]:
                   e.dma_start(out=o, in_=i), w if h == 0 else [],
                   (sem, 16), f"dtr0.{sp}.{h}")
        dtr_pre[sp] = ((f"dtrh{sp}", 16), (f"dtr{p}", dtr_cnt0[p]))

    v_G = [0] * T
    gw_after = [0] * T
    mir_m = [0]
    n_mirrors = sum(T - 4 * (et // 4 + 1) for et in range(T) if 4 * (et // 4 + 1) < T)

    def emit_mirror(et, yt):
        m = mir_m[0]
        mir_m[0] += 1
        b = m % 2
        w = [(f"gw{yt % 2}", gw_after[yt])]
        if m >= 2:
            w.append(("mirW", 16 * (m - 1)))
        P.emit("scalar", lambda e, o=mst[b][:],
               i=d_G[et // 2, :, yt, (et % 2) * 128:(et % 2) * 128 + 128]:
               e.dma_start_transpose(out=o, in_=i), w, ("mirR", 16), f"mirR{m}")
        P.emit("sync", lambda e,
               o=d_G[yt // 2, :, et, (yt % 2) * 128:(yt % 2) * 128 + 128],
               i=mst[b][:]:
               e.dma_start(out=o, in_=i), [("mirR", 16 * (m + 1))], ("mirW", 16), f"mirW{m}")

    late_mirrors = []
    for et in range(T):
        p = et % 2
        c_hi = et // 4 + 1
        first = [("v", v_ln_last), pfree[p]]
        for c in range(c_hi):
            for sc in range(T):
                stop = (sc == T - 1)
                inc = ("t", 1) if (stop and c == c_hi - 1) else None
                P.emit("tensor", MM(None, ps[p][:, c * 512:(c + 1) * 512],
                                    xn_sb[:, sc * 2048 + et * 128: sc * 2048 + (et + 1) * 128],
                                    xn_sb[:, sc * 2048 + c * 512: sc * 2048 + (c + 1) * 512],
                                    sc == 0, stop),
                       first if (sc == 0 and c == 0) else [], inc, f"G{et}")
        tg = P.cnt["t"]
        pready[p] = tg
        w = [("t", tg)]
        if et < 2:
            w += dtw_all
        if et >= 2:
            w.append((f"gw{p}", gw_after[et - 2]))
        cv = P.emit("vector", lambda e, o=Gst[p][:, :c_hi * 512], i=ps[p][:, :c_hi * 512]:
                    e.tensor_copy(o, i), w, ("v", 1), f"Gev{et}")
        v_G[et] = cv
        pfree[p] = ("v", cv)
        P.emit("sync", lambda e,
               o=d_G[:2 * c_hi, :, et, :].rearrange("ep p j -> p ep j"),
               i=Gst[p][:, :c_hi * 512].rearrange("p (ep j) -> p ep j", ep=2 * c_hi):
               e.dma_start(out=o, in_=i), [("v", cv)], (f"gw{p}", 16), f"Gsp{et}")
        gw_after[et] = P.cnt[f"gw{p}"]
        if et >= 4:
            tgts = [et2 for et2 in range(4 * (et // 4)) if et >= 4 * (et2 // 4 + 1)]
            if et < 8:
                for et2 in tgts:
                    emit_mirror(et2, et)
            else:
                late_mirrors.extend((et2, et) for et2 in tgts)
    t_G_last = pready[(T - 1) % 2]
    gw_all = [("gw0", P.cnt["gw0"]), ("gw1", P.cnt["gw1"])]
    mir_all = [("mirW", 16 * n_mirrors)]


    load_w1q_seg(0, 60, 64, gw_all)
    P.emit("gpsimd", lambda e, o=b1_sb[:], i=d_b1: e.dma_start(out=o, in_=i),
           [("t", t_L1_last)], ("b1", 16), "b1load")
    load_w1q_seg(1, 0, NCH, [("t", t_G_last)])

    t_L2 = {}
    v_bias_last = 0
    ggw_cnt = [0, 0]
    dtr_cnt = [0, 0]
    s_silu = {}
    for q in range(HQ):
        qb = w1q[0] if q % 2 == 0 else w1q[1]
        for sp in range(SPP):
            p = sp % 2
            split = (q == 0 and sp < 2)
            if split:
                dtr_cnt[p] = dtr_cnt0[p]
                dw_half, dw = dtr_pre[sp]
            else:
                prev = (q, sp - 2) if sp >= 2 else (q - 1, SPP - 2 + sp)
                w = [("t", t_L2[prev])]
                dtr_cnt[p] += 16
                P.emit("sync", lambda e, o=dts[p][:].rearrange("p (n s) -> p n s", n=NCH),
                       i=d_dt[sp]:
                       e.dma_start(out=o, in_=i), w, (f"dtr{p}", 16), f"dtr{q}.{sp}")
                dw_half = dw = (f"dtr{p}", dtr_cnt[p])
            qrdy = w1q0_partial if q == 0 else w1q_ready[q]
            first = [dw_half, (f"w1q{q}", qrdy), pfree[p]]
            for j in range(2):
                for n in range(NCH):
                    stop = (n == NCH - 1)
                    inc = ("t", 1) if (stop and j == 1) else None
                    ww = []
                    if n == 0 and j == 0:
                        ww = first
                    elif split and j == 0 and n == 32:
                        ww = [dw, dw_half]
                    if q == 0 and sp < 2 and j == 0 and n == 60:
                        ww = ww + [("w1q0", w1q_ready[0])]
                    P.emit("tensor", MM(None, ps[p][:, j * 512:(j + 1) * 512],
                                        dts[p][:, n * 256 + j * 128: n * 256 + (j + 1) * 128],
                                        qb[:, n * 512:(n + 1) * 512],
                                        n == 0, stop),
                           ww, inc, f"L2_{q}.{sp}")
            tg = P.cnt["t"]
            t_L2[(q, sp)] = tg
            pready[p] = tg
            last_v = 0
            for j in range(2):
                w = [("t", tg), ("b1", 16)]
                if q == 0 and sp == 0:
                    w = w + gw_all
                key = (q, sp - 1) if sp >= 1 else (q - 1, SPP - 1)
                if key in s_silu:
                    w.append(("s", s_silu[key][j]))
                cv = P.emit("vector", lambda e, o=e_st[j][:],
                            i=ps[p][:, j * 512:(j + 1) * 512], b=b1_sb[:, q * 512:(q + 1) * 512]:
                            e.tensor_tensor(o, i, b, ALU.add),
                            w, ("v", 1), f"bias{q}.{sp}.{j}")
                last_v = cv
                w2 = [("v", cv)]
                if ggw_cnt[j] > 0:
                    w2.append((f"ggw{j}", ggw_cnt[j]))
                cs = P.emit("scalar", lambda e, o=gst[j][:], i=e_st[j][:]:
                            e.activation(o, i, AF.Silu), w2, ("s", 1), f"silu{q}.{sp}.{j}")
                s_silu.setdefault((q, sp), {})[j] = cs
                ggw_cnt[j] += 16
                P.emit("scalar", lambda e, o=d_g[sp * 2 + j, :, q * 512:(q + 1) * 512],
                       i=gst[j][:]: e.dma_start(out=o, in_=i),
                       [("s", cs)], (f"ggw{j}", 16), f"gsp{q}.{sp}.{j}")
            pfree[p] = ("v", last_v)
            v_bias_last = last_v
            for _ in range(3):
                if late_mirrors:
                    emit_mirror(*late_mirrors.pop(0))
        if q + 2 < HQ:
            load_w1q(q + 2, [("t", t_L2[(q, SPP - 1)])])
    while late_mirrors:
        emit_mirror(*late_mirrors.pop(0))
    t_L2_last = t_L2[(HQ - 1, SPP - 1)]
    ggw_all = [("ggw0", P.cnt["ggw0"]), ("ggw1", P.cnt["ggw1"])]
    s_silu_last = s_silu[(HQ - 1, SPP - 1)][1]

    for c4 in range(4):
        P.emit("gpsimd", lambda e,
               o=wk_sb[:].rearrange("p (c f) -> p c f", c=T)[:, c4 * 4:(c4 + 1) * 4, :],
               i=d_wk.rearrange("(c p) f -> p c f", p=128)[:, c4 * 4:(c4 + 1) * 4, :]:
               e.dma_start(out=o, in_=i),
               [("t", t_L2[(2, SPP - 1)])] if c4 == 0 else [], ("wk", 16), f"wkl{c4}")
    wk_all = P.cnt["wk"]

    v_T1 = [0] * T
    t_T1_group = {}
    gr_cnt = [0, 0]
    for et in range(T):
        ep, j = et // 2, et % 2
        p2 = ep % 2
        if j == 0:
            if ep < 2:
                w = [("t", t_L2_last)] + gw_all + mir_all
            else:
                w = [("t", t_T1_group[2 * (ep - 2) + 1])]
            gr_cnt[p2] += 16
            P.emit("sync", lambda e, o=gpr[p2][:], i=d_G[ep]:
                   e.dma_start(out=o, in_=i.rearrange("p t j -> p (t j)")),
                   w, (f"gr{p2}", 16), f"gpr{ep}")
        grw = (f"gr{p2}", gr_cnt[p2])
        p = et % 2
        first = [grw, ("wk", wk_all), pfree[p]]
        for c in range(C4):
            for e2 in range(T):
                stop = (e2 == T - 1)
                inc = ("t", 1) if (stop and c == C4 - 1) else None
                P.emit("tensor", MM(None, ps[p][:, c * 512:(c + 1) * 512],
                                    gpr[p2][:, e2 * 256 + j * 128: e2 * 256 + (j + 1) * 128],
                                    wk_sb[:, e2 * 2048 + c * 512: e2 * 2048 + (c + 1) * 512],
                                    e2 == 0, stop),
                       first if (e2 == 0 and c == 0) else [], inc, f"T1_{et}")
        tg = P.cnt["t"]
        pready[p] = tg
        t_T1_group[et] = tg
        w = [("t", tg)]
        if et < 2:
            w += dtw_all
        cv = P.emit("vector", lambda e, o=t1_sb[:, et * 2048:(et + 1) * 2048], i=ps[p][:]:
                    e.tensor_copy(o, i), w, ("v", 1), f"T1ev{et}")
        v_T1[et] = cv
        pfree[p] = ("v", cv)
    t_T1_last = t_T1_group[T - 1]
    v_T1_last = v_T1[T - 1]

    for e_ in range(T):
        P.emit("sync", lambda e, o=xnt2[e_][:], i=d_xnt[e_]:
               e.dma_start(out=o, in_=i),
               [("t", t_T1_last)] if e_ == 0 else [], ("xnt2", 16), f"xnt2_{e_}")
    xnt2_all = P.cnt["xnt2"]

    ws_cnt = [0, 0, 0]
    ws_hcnt = [0, 0, 0]
    ws_user = {}

    def slice_load(gen, queue, fn, data_waits, tag):
        b = gen % 3
        kind = "wsg" if queue == "gpsimd" else "wsh"
        w = list(data_waits)
        if gen >= 3:
            w.append(("t", ws_user[gen - 3]))
        if gen < 3:
            if b < 2:
                w.append(("v", v_bias_last))
            else:
                w.append(("s", s_silu_last))
                w.extend(ggw_all)
        ws_cnt[b] = ws_cnt[b] + 16 if kind == "wsg" else ws_cnt[b]
        if kind == "wsg":
            P.emit(queue, fn, w, (f"wsg{b}", 16), tag)
            return (f"wsg{b}", ws_cnt[b])
        ws_hcnt[b] += 16
        P.emit(queue, fn, w, (f"wsh{b}", 16), tag)
        return (f"wsh{b}", ws_hcnt[b])

    s_exp = [0] * T
    v_A = [0] * T
    aw_cnt = [0, 0]
    t_scores_group = {}
    for et in range(T):
        p = et % 2
        b = et % 3
        wqw = slice_load(et, "gpsimd",
                         lambda e, o=wsb[b][:].rearrange("p (c m) -> p c m", c=T),
                         i=d_wq[et]: e.dma_start(out=o, in_=i), [], f"wql{et}")
        first = [wqw, ("v", v_T1_last), pfree[p]]
        for c in range(C4):
            for e1 in range(T):
                stop = (e1 == T - 1)
                inc = ("t", 1) if (stop and c == C4 - 1) else None
                P.emit("tensor", MM(None, ps[p][:, c * 512:(c + 1) * 512],
                                    wsb[b][:, e1 * 128:(e1 + 1) * 128],
                                    t1_sb[:, e1 * 2048 + c * 512: e1 * 2048 + (c + 1) * 512],
                                    e1 == 0, stop),
                       first if (e1 == 0 and c == 0) else [], inc, f"SC{et}")
        tg = P.cnt["t"]
        t_scores_group[et] = tg
        ws_user[et] = tg
        pready[p] = tg
        c1 = P.emit("vector", lambda e, o=smx[:], i=ps[p][:]:
                    e.reduce_max(o, i, axis=AX.X), [("t", tg)], ("v", 1), f"smax{et}")
        c2 = P.emit("vector", lambda e, o=snmx[:], i=smx[:]:
                    e.tensor_scalar(o, i, -0.125, None, op0=ALU.mult),
                    [("v", c1)], ("v", 1), f"snmx{et}")
        w = [("v", c2)]
        if et >= 1:
            w.append(("v", v_A[et - 1]))
        c3 = P.emit("scalar", lambda e, o=expb[:], i=ps[p][:], b=snmx[:], a=srsm[:]:
                    e.activation(o, i, AF.Exp, bias=b, scale=0.125, accum_out=a),
                    w, ("s", 1), f"sexp{et}")
        s_exp[et] = c3
        pfree[p] = ("s", c3)
        c4 = P.emit("vector", lambda e, o=srrc[:], i=srsm[:]:
                    e.reciprocal(o, i), [("s", c3)], ("v", 1), f"srrc{et}")
        w = [("v", c4)]
        if et >= 2:
            w.append((f"aw{p}", 16 * (et // 2)))
        c5 = P.emit("vector", lambda e, o=Ast[p][:], i=expb[:], r=srrc[:]:
                    e.tensor_scalar(o, i, r, None, op0=ALU.mult),
                    w, ("v", 1), f"sA{et}")
        v_A[et] = c5
        aw_cnt[p] += 16
        P.emit("scalar", lambda e, o=d_A[et], i=Ast[p][:]:
               e.dma_start(out=o, in_=i), [("v", c5)], (f"aw{p}", 16), f"Asp{et}")
    t_scores_last = t_scores_group[T - 1]
    aw_all = [("aw0", P.cnt["aw0"]), ("aw1", P.cnt["aw1"])]

    if debug:
        P.emit("sync", lambda e, o=dbg["t1_dbg"], i=t1_sb[:]: e.dma_start(out=o, in_=i),
               [("v", v_T1_last)], ("dbgw", 16), "dbg_t1")

    v_VT = [0] * T
    t_VT_group = {}
    for ft in range(T):
        p = ft % 2
        b = (16 + ft) % 3
        wvw = slice_load(16 + ft, "gpsimd",
                         lambda e, o=wsb[b][:].rearrange("p (c m) -> p c m", c=T),
                         i=d_wv[ft]: e.dma_start(out=o, in_=i), [], f"wvl{ft}")
        if ft in (3, 5, 7, 9):
            c4 = (ft - 3) // 2
            w_ = [("t", t_scores_last)] + ([("dbgw", 16)] if debug else [])
            P.emit("gpsimd", lambda e,
                   o=wo_sb[:].rearrange("p (c f) -> p c f", c=T)[:, c4 * 4:(c4 + 1) * 4, :],
                   i=d_wo.rearrange("(c p) f -> p c f", p=128)[:, c4 * 4:(c4 + 1) * 4, :]:
                   e.dma_start(out=o, in_=i),
                   w_ if c4 == 0 else [], ("wo", 16), f"wol{c4}")
        first = [wvw, ("xnt2", xnt2_all), pfree[p]]
        for c in range(C4):
            for ec in range(T):
                stop = (ec == T - 1)
                inc = ("t", 1) if (stop and c == C4 - 1) else None
                P.emit("tensor", MM(None, ps[p][:, c * 512:(c + 1) * 512],
                                    wsb[b][:, ec * 128:(ec + 1) * 128],
                                    xnt2[ec][:, c * 512:(c + 1) * 512],
                                    ec == 0, stop),
                       first if (ec == 0 and c == 0) else [], inc, f"VT{ft}")
        tg = P.cnt["t"]
        t_VT_group[ft] = tg
        ws_user[16 + ft] = tg
        pready[p] = tg
        w = [("t", tg)]
        if ft == 0:
            w.append(("v", v_A[T - 1]))
        if ft == 2:
            w.append(("aw0", 16 * (T // 2)))
        if ft == 3:
            w.append(("aw1", 16 * (T // 2)))
        cv = P.emit("vector", lambda e, o=vt_sb[:, ft * 2048:(ft + 1) * 2048], i=ps[p][:]:
                    e.tensor_copy(o, i), w, ("v", 1), f"VTev{ft}")
        v_VT[ft] = cv
        pfree[p] = ("v", cv)
    t_VT_last = t_VT_group[T - 1]
    wo_all = P.cnt["wo"]

    v_B = [0] * T
    t_B_group = {}
    for ft in range(T):
        p = ft % 2
        b = (32 + ft) % 3
        arw = slice_load(32 + ft, "sync" if ft % 2 == 0 else "scalar",
                         lambda e, o=wsb[b][:].rearrange("p (t m) -> p t m", t=T),
                         i=d_A[:, :, ft * 128:(ft + 1) * 128].rearrange("t k m -> k t m"):
                         e.dma_start(out=o, in_=i), aw_all, f"Ard{ft}")
        first = [arw, ("wo", wo_all), pfree[p]]
        for c in range(C4):
            for ec in range(T):
                stop = (ec == T - 1)
                inc = ("t", 1) if (stop and c == C4 - 1) else None
                P.emit("tensor", MM(None, ps[p][:, c * 512:(c + 1) * 512],
                                    wsb[b][:, ec * 128:(ec + 1) * 128],
                                    wo_sb[:, ec * 2048 + c * 512: ec * 2048 + (c + 1) * 512],
                                    ec == 0, stop),
                       first if (ec == 0 and c == 0) else [], inc, f"B{ft}")
        tg = P.cnt["t"]
        t_B_group[ft] = tg
        ws_user[32 + ft] = tg
        pready[p] = tg
        cv = P.emit("vector", lambda e, o=b_sb[:, ft * 2048:(ft + 1) * 2048], i=ps[p][:]:
                    e.tensor_copy(o, i), [("t", tg)], ("v", 1), f"Bev{ft}")
        v_B[ft] = cv
        pfree[p] = ("v", cv)
    t_B_last = t_B_group[T - 1]
    v_B_last = v_B[T - 1]

    if debug:
        P.emit("sync", lambda e, o=dbg["vt_dbg"], i=vt_sb[:]: e.dma_start(out=o, in_=i),
               [("v", v_VT[T - 1])], ("dbgw", 16), "dbg_vt")
        P.emit("sync", lambda e, o=dbg["b_dbg"], i=b_sb[:]: e.dma_start(out=o, in_=i),
               [("v", v_B_last)], ("dbgw", 16), "dbg_b")

    v_cmb = [0] * T
    for st_ in range(T):
        p = st_ % 2
        w = [("t", t_B_last)] if st_ < 2 else [("v", v_cmb[st_ - 2])]
        P.emit("sync", lambda e, o=xb2[p][:], i=d_x[st_ * 128:(st_ + 1) * 128, :]:
               e.dma_start(out=o, in_=i), w, (f"x2_{p}", 16), f"x2l{st_}")
        wg = list(w)
        if st_ < 2:
            wg += ggw_all
        P.emit("sync", lambda e, o=gb[p][:], i=d_g[st_]:
               e.dma_start(out=o, in_=i), wg, (f"gl{p}", 16), f"gl{st_}")
        xw = (f"x2_{p}", P.cnt[f"x2_{p}"])
        gw_ = (f"gl{p}", P.cnt[f"gl{p}"])
        first = [("v", v_B_last), pfree[p]]
        for c in range(C4):
            for f in range(T):
                stop = (f == T - 1)
                inc = ("t", 1) if (stop and c == C4 - 1) else None
                P.emit("tensor", MM(None, ps[p][:, c * 512:(c + 1) * 512],
                                    vt_sb[:, f * 2048 + st_ * 128: f * 2048 + (st_ + 1) * 128],
                                    b_sb[:, f * 2048 + c * 512: f * 2048 + (c + 1) * 512],
                                    f == 0, stop),
                       first if (f == 0 and c == 0) else [], inc, f"SA{st_}")
        tg = P.cnt["t"]
        pready[p] = tg
        c1 = P.emit("vector", lambda e, o=tmpb[:], i=ps[p][:], g_=gb[p][:]:
                    e.tensor_tensor(o, i, g_, ALU.mult),
                    [("t", tg), gw_], ("v", 1), f"cmul{st_}")
        w = [("v", c1), xw]
        if st_ >= 2:
            w.append((f"outw{p}", 16 * (st_ // 2)))
        c2 = P.emit("vector", lambda e, o=ob[p][:], i=tmpb[:], x_=xb2[p][:]:
                    e.tensor_tensor(o, i, x_, ALU.add), w, ("v", 1), f"cadd{st_}")
        v_cmb[st_] = c2
        pfree[p] = ("v", c1)
        P.emit("sync", lambda e, o=d_out[st_ * 128:(st_ + 1) * 128, :], i=ob[p][:]:
               e.dma_start(out=o, in_=i), [("v", c2)], (f"outw{p}", 16), f"outw{st_}")
    P.emit("sync", lambda e: e.nop(),
           [("outw0", P.cnt["outw0"]), ("outw1", P.cnt["outw1"])], None, "final")

    P.verify()

    with ExitStack() as stx:
        sems = {n: stx.enter_context(nc.semaphore(n)) for n in sorted(P.cnt)}
        block = stx.enter_context(nc.Block())

        def replay(name):
            def f(eng):
                for fn, ws, inc, tag in P.q[name]:
                    for s, v in ws:
                        eng.wait_ge(sems[s], v)
                    ins = fn(eng)
                    if inc is not None:
                        ins.then_inc(sems[inc[0]], inc[1])
            return f

        block.tensor(replay("tensor"))
        block.vector(replay("vector"))
        block.scalar(replay("scalar"))
        block.gpsimd(replay("gpsimd"))
        block.sync(replay("sync"))
    return nc


def prep_inputs(L0_attn_x, L0_attn_Wq, L0_attn_Wk, L0_attn_Wv, L0_attn_Wo,
                L0_ffn_W0, L0_ffn_bias0, L0_ffn_W1, L0_ffn_bias1):
    x = np.asarray(L0_attn_x, dtype=np.float32)
    Wq = np.asarray(L0_attn_Wq, dtype=np.float32)
    Wk = np.asarray(L0_attn_Wk, dtype=np.float32)
    Wv = np.asarray(L0_attn_Wv, dtype=np.float32)
    Wo = np.asarray(L0_attn_Wo, dtype=np.float32)
    W0 = np.asarray(L0_ffn_W0, dtype=np.float32)
    b0 = np.asarray(L0_ffn_bias0, dtype=np.float32)
    W1 = np.asarray(L0_ffn_W1, dtype=np.float32)
    b1 = np.asarray(L0_ffn_bias1, dtype=np.float32)

    wq_blk = np.ascontiguousarray(Wq.reshape(T, 128, T, 128).transpose(2, 1, 0, 3))
    wv_blk = np.ascontiguousarray(Wv.reshape(T, 128, T, 128).transpose(2, 1, 0, 3))
    w0t_blk = np.ascontiguousarray(W0.T.reshape(T, 128, NCH, 128).transpose(2, 1, 0, 3))
    w1t = np.ascontiguousarray(W1.T)
    b0t = np.ascontiguousarray(b0.reshape(NCH, 128).T)
    b1r = np.ascontiguousarray(np.broadcast_to(b1, (128, 2048)))
    shared = {
        "wq_blk": wq_blk, "wk": np.ascontiguousarray(Wk), "wv_blk": wv_blk,
        "wo": np.ascontiguousarray(Wo), "w0t_blk": w0t_blk, "w1t": w1t,
        "bias0_t": b0t, "bias1_rep": b1r,
        "iden": np.eye(128, dtype=np.float32),
    }
    return [dict(shared, x=np.ascontiguousarray(x[c])) for c in range(8)]


def kernel(**inputs):
    in_maps = prep_inputs(**inputs)
    nc = build_nc(debug=False)
    res = run_bass_kernel_spmd(nc, in_maps, core_ids=list(range(8)))
    return np.stack([np.asarray(res.results[c]["out"], dtype=np.float32)
                     for c in range(8)], axis=0)


# version 18
# speedup vs baseline: 1.2490x; 1.2490x over previous
import numpy as np
from contextlib import ExitStack

import concourse.bass as bass
import concourse.mybir as mybir
from concourse.bass_utils import run_bass_kernel_spmd

F32 = mybir.dt.float32
BF16 = mybir.dt.bfloat16
AF = mybir.ActivationFunctionType
ALU = mybir.AluOpType
AX = mybir.AxisListType

T = 16
C4 = 4
NCH = 64
HQ = 4
SPP = 8

ENGS = ("tensor", "vector", "scalar", "gpsimd", "sync")

OFF_S = 16640
OFF_SLOT0 = 32736
OFF_SLOT1 = 98272
OFF_SLOT2 = 163808


class Plan:
    def __init__(self):
        self.q = {e: [] for e in ENGS}
        self.cnt = {}
        self._lastwait = {e: {} for e in ENGS}

    def emit(self, eng, fn, waits=(), inc=None, tag=""):
        ws = []
        for s, v in waits:
            if v is None or v <= 0:
                continue
            self.cnt.setdefault(s, 0)
            if self._lastwait[eng].get(s, 0) >= v:
                continue
            self._lastwait[eng][s] = v
            ws.append((s, v))
        after = None
        if inc is not None:
            s, n = inc
            self.cnt[s] = self.cnt.get(s, 0) + n
            after = self.cnt[s]
        self.q[eng].append((fn, tuple(ws), inc, tag))
        return after

    def verify(self):
        ptr = {e: 0 for e in ENGS}
        cnt = {s: 0 for s in self.cnt}
        progress = True
        while progress:
            progress = False
            for e in ENGS:
                while ptr[e] < len(self.q[e]):
                    fn, ws, inc, tag = self.q[e][ptr[e]]
                    if any(cnt[s] < v for s, v in ws):
                        break
                    if inc is not None:
                        cnt[inc[0]] += inc[1]
                    ptr[e] += 1
                    progress = True
        stuck = {e: (ptr[e], len(self.q[e])) for e in ENGS if ptr[e] < len(self.q[e])}
        if stuck:
            lines = []
            for e, (p, n) in stuck.items():
                fn, ws, inc, tag = self.q[e][p]
                missing = [(s, v, cnt.get(s, 0)) for s, v in ws if cnt.get(s, 0) < v]
                lines.append(f"{e} stuck at {p}/{n} tag={tag} missing={missing}")
            raise RuntimeError("plan deadlock:\n" + "\n".join(lines))


def build_nc(debug=False):
    nc = bass.Bass("TRN2", target_bir_lowering=False)

    d_x = nc.dram_tensor("x", [2048, 2048], F32, kind="ExternalInput").ap()
    d_wq = nc.dram_tensor("wq_blk", [T, 128, T, 128], F32, kind="ExternalInput").ap()
    d_wk = nc.dram_tensor("wk", [2048, 2048], F32, kind="ExternalInput").ap()
    d_wv = nc.dram_tensor("wv_blk", [T, 128, T, 128], F32, kind="ExternalInput").ap()
    d_wo = nc.dram_tensor("wo", [2048, 2048], F32, kind="ExternalInput").ap()
    d_w0 = nc.dram_tensor("w0t_blk", [NCH, 128, T, 128], F32, kind="ExternalInput").ap()
    d_w1 = nc.dram_tensor("w1t", [8192, 2048], F32, kind="ExternalInput").ap()
    d_b0 = nc.dram_tensor("bias0_t", [128, NCH], F32, kind="ExternalInput").ap()
    d_b1 = nc.dram_tensor("bias1_rep", [128, 2048], F32, kind="ExternalInput").ap()
    d_idn = nc.dram_tensor("iden", [128, 128], F32, kind="ExternalInput").ap()
    d_out = nc.dram_tensor("out", [2048, 2048], F32, kind="ExternalOutput").ap()

    skind = "ExternalOutput" if debug else "Internal"
    d_xnt = nc.dram_tensor("xnt_hbm", [T, 128, 2048], BF16, kind=skind).ap()
    d_G = nc.dram_tensor("G_hbm", [SPP, 128, T, 256], BF16, kind=skind).ap()
    d_A = nc.dram_tensor("A_hbm", [T, 128, 2048], BF16, kind=skind).ap()
    d_dt = nc.dram_tensor("dt_hbm", [SPP, 128, NCH, 256], BF16, kind=skind).ap()
    d_g = nc.dram_tensor("g_hbm", [T, 128, 2048], BF16, kind=skind).ap()
    dbg = {}
    if debug:
        for nm in ("t1_dbg", "b_dbg", "vt_dbg", "xn_dbg", "xn2_dbg", "xnt1_dbg"):
            dbg[nm] = nc.dram_tensor(nm, [128, T * 2048], BF16, kind="ExternalOutput").ap()

    def sb(name, off, shape, dtype):
        return nc.alloc_sbuf_tensor_at(name, shape, dtype, offset=off)

    xb = [sb(f"xb{i}", OFF_SLOT1 + i * 8192, [128, 2048], F32) for i in range(4)]
    STT = OFF_S + 14336
    bnst = [sb(f"bnst{i}", STT + i * 256, [128, 24], F32) for i in range(2)]
    agg = [sb(f"agg{i}", STT + 96 + i * 256, [128, 2], F32) for i in range(2)]
    stdv = [sb(f"stdv{i}", STT + 128 + i * 256, [128, 1], F32) for i in range(2)]
    rstd = [sb(f"rstd{i}", STT + 160 + i * 256, [128, 1], F32) for i in range(2)]
    eps_sb = sb("eps_sb", STT + 192, [128, 1], F32)
    idn_sb = sb("idn_sb", STT + 512, [128, 128], BF16)
    b0_sb = sb("b0_sb", STT + 768, [128, NCH], F32)
    mst = [sb("mst0", STT + 1024, [128, 128], BF16),
           sb("mst1", STT + 1280, [128, 128], BF16)]
    xn_sb = sb("xn_sb", OFF_SLOT0, [128, T * 2048], BF16)
    wsb = [sb(f"wsb{i}", OFF_S + i * 4096, [128, 2048], BF16) for i in range(3)]
    Gst = [sb("Gst0", OFF_S + 8192, [128, 2048], BF16),
           sb("Gst1", OFF_SLOT1 + 61440, [128, 2048], BF16)]
    xnt1 = sb("xnt1", OFF_SLOT2, [128, T * 2048], BF16)
    w0b = [sb("w0b0", OFF_S, [128, 2048], BF16),
           sb("w0b1", OFF_S + 4096, [128, 2048], BF16)]
    dt_st = [sb("dtst0", OFF_S + 8192, [128, 2048], BF16),
             sb("dtst1", OFF_SLOT1 + 61440, [128, 2048], BF16)]
    w1q = [sb("w1q_a", OFF_SLOT1, [128, NCH * 512], BF16),
           sb("w1q_b", OFF_SLOT0, [128, NCH * 512], BF16)]
    dts = [sb("dts0", OFF_SLOT2, [128, NCH * 256], BF16),
           sb("dts1", OFF_SLOT2 + 32768, [128, NCH * 256], BF16)]
    b1_sb = sb("b1_sb", OFF_S, [128, 2048], F32)
    e_st = [sb("est0", OFF_S + 10240, [128, 512], F32),
            sb("est1", OFF_S + 12288, [128, 512], F32)]
    gst = [sb("gst0", OFF_S + 8192, [128, 512], BF16),
           sb("gst1", OFF_S + 9216, [128, 512], BF16)]
    wk_sb = sb("wk_sb", OFF_SLOT1, [128, T * 2048], BF16)
    gpr = [sb("gpr0", OFF_SLOT2, [128, T * 256], BF16),
           sb("gpr1", OFF_SLOT2 + 8192, [128, T * 256], BF16)]
    t1_sb = sb("t1_sb", OFF_SLOT0, [128, T * 2048], BF16)
    expb = sb("expb", OFF_SLOT2, [128, 2048], F32)
    Ast = [sb("Ast0", OFF_SLOT2 + 8192, [128, 2048], BF16),
           sb("Ast1", OFF_SLOT2 + 12288, [128, 2048], BF16)]
    smx = sb("smx", STT, [128, 1], F32)
    snmx = sb("snmx", STT + 32, [128, 1], F32)
    srsm = sb("srsm", STT + 64, [128, 1], F32)
    srrc = sb("srrc", STT + 96, [128, 1], F32)
    xnt2 = [sb(f"xnt2_{e}", OFF_SLOT1 + e * 4096, [128, 2048], BF16) for e in range(T)]
    vt_sb = sb("vt_sb", OFF_SLOT2, [128, T * 2048], BF16)
    wo_sb = sb("wo_sb", OFF_SLOT0, [128, T * 2048], BF16)
    b_sb = sb("b_sb", OFF_SLOT1, [128, T * 2048], BF16)
    xb2 = [sb("xb2_0", OFF_SLOT0, [128, 2048], F32),
           sb("xb2_1", OFF_SLOT0 + 8192, [128, 2048], F32)]
    gb = [sb("gb0", OFF_SLOT0 + 16384, [128, 2048], BF16),
          sb("gb1", OFF_SLOT0 + 20480, [128, 2048], BF16)]
    tmpb = sb("tmpb", OFF_SLOT0 + 24576, [128, 2048], F32)
    ob = [sb("ob0", OFF_SLOT0 + 32768, [128, 2048], F32),
          sb("ob1", OFF_SLOT0 + 40960, [128, 2048], F32)]

    ps = [nc.alloc_psum_tensor("pA", [128, 2048], F32),
          nc.alloc_psum_tensor("pB", [128, 2048], F32)]

    P = Plan()
    pfree = [("v", 0), ("v", 0)]
    pready = [0, 0]

    def MM(eng_unused, out, lhsT, rhs, start, stop):
        return lambda e, o=out, l=lhsT, r=rhs, st=start, sp=stop: e.matmul(
            o, l, r, start=st, stop=sp)

    v_eps = P.emit("vector", lambda e, o=eps_sb[:]: e.memset(o, 1e-5),
                   [], ("v", 1), "eps")
    P.emit("gpsimd", lambda e, o=idn_sb[:], i=d_idn: e.dma_start(out=o, in_=i),
           [], ("idn", 16), "idnload")
    P.emit("gpsimd", lambda e, o=b0_sb[:], i=d_b0: e.dma_start(out=o, in_=i),
           [], ("b0", 16), "b0load")

    vA = [0] * T
    sB = [0] * T
    vC = [0] * T
    tT = [0] * T
    vE = [0] * T

    def emit_x(st_):
        px = st_ % 4
        waits = []
        if st_ >= 4:
            waits = [("s", sB[st_ - 4]), ("v", vC[st_ - 4])]
        P.emit("sync", lambda e, o=xb[px][:], i=d_x[st_ * 128:(st_ + 1) * 128, :]:
               e.dma_start(out=o, in_=i), waits, (f"x{px}", 16), f"xload{st_}")

    def emit_A(st_):
        p = st_ % 2
        px = st_ % 4
        xw = (f"x{px}", 16 * (st_ // 4 + 1))
        w = [xw]
        if st_ >= 2:
            w.append(("s", sB[st_ - 2]))
        for c in range(4):
            inc = ("v", 1) if c == 3 else None
            P.emit("vector", lambda e, o=bnst[p][:, c * 6:(c + 1) * 6],
                   i=xb[px][:, c * 512:(c + 1) * 512]:
                   e.bn_stats(o, i), w if c == 0 else [], inc, f"lnbn{st_}")
        vA[st_] = P.emit("vector", lambda e, o=agg[p][:], i=bnst[p][:]:
                         e.bn_aggr(o, i), [("v", P.cnt["v"])], ("v", 1), f"lnag{st_}")

    def emit_B(st_):
        p = st_ % 2
        w = [("v", vA[st_])]
        if st_ >= 2:
            w.append(("v", vC[st_ - 2]))
        sB[st_] = P.emit("scalar", lambda e, o=stdv[p][:], i=agg[p][:, 1:2], b=eps_sb[:]:
                         e.activation(o, i, AF.Sqrt, bias=b),
                         w + [("v", v_eps)], ("s", 1), f"lnsd{st_}")

    def emit_C(st_):
        p = st_ % 2
        px = st_ % 4
        c5 = P.emit("vector", lambda e, o=rstd[p][:], i=stdv[p][:]:
                    e.reciprocal(o, i), [("s", sB[st_])], ("v", 1), f"lnrs{st_}")
        vC[st_] = P.emit("vector",
                         lambda e, o=xn_sb[:, st_ * 2048:(st_ + 1) * 2048], i=xb[px][:],
                         m=agg[p][:, 0:1], r=rstd[p][:]:
                         e.tensor_scalar(o, i, m, r, op0=ALU.subtract, op1=ALU.mult),
                         [("v", c5)], ("v", 1), f"lnxn{st_}")

    def emit_T(st_):
        p = st_ % 2
        first = [("v", vC[st_]), ("idn", 16), pfree[p]]
        for et in range(T):
            inc = ("t", 1) if et == T - 1 else None
            P.emit("tensor", MM(None, ps[p][:, et * 128:(et + 1) * 128],
                                xn_sb[:, st_ * 2048 + et * 128: st_ * 2048 + (et + 1) * 128],
                                idn_sb[:], et % 4 == 0, et % 4 == 3),
                   first if et == 0 else [], inc, f"TP{st_}")
        tT[st_] = P.cnt["t"]
        pready[p] = tT[st_]

    def emit_E(st_):
        p = st_ % 2
        vE[st_] = P.emit(
            "scalar",
            lambda e,
            o=xnt1[:].rearrange("p (et s2) -> p et s2", et=T)[:, :, st_ * 128:(st_ + 1) * 128],
            i=ps[p][:].rearrange("p (et s) -> p et s", et=T):
            e.copy(o, i), [("t", tT[st_])], ("s", 1), f"TPev{st_}")
        pfree[p] = ("s", vE[st_])

    for i in range(4):
        emit_x(i)
    emit_A(0)
    emit_B(0)
    emit_A(1)
    emit_B(1)
    for st_ in range(T):
        emit_C(st_)
        emit_T(st_)
        emit_E(st_)
        if st_ + 4 < T:
            emit_x(st_ + 4)
        if st_ + 2 < T:
            emit_A(st_ + 2)
            emit_B(st_ + 2)
    v_ln_last = vC[T - 1]
    s_sq_last = sB[T - 1]
    vE_last = vE[T - 1]
    if debug:
        P.emit("sync", lambda e, o=dbg["xn_dbg"], i=xn_sb[:]: e.dma_start(out=o, in_=i),
               [("v", v_ln_last)], ("dbgx", 16), "dbg_xn")

    for e_ in range(T):
        P.emit("scalar", lambda e, o=d_xnt[e_], i=xnt1[:, e_ * 2048:(e_ + 1) * 2048]:
               e.dma_start(out=o, in_=i),
               [("s", vE_last)], ("xtw", 16), f"xtspill{e_}")
    xtw_all = P.cnt["xtw"]

    w1q_ready = [0] * HQ

    def load_w1q_seg(q, n0, n1, waits):
        buf = w1q[0] if q % 2 == 0 else w1q[1]
        step = 8
        for s0 in range(n0, n1, step):
            s1 = min(s0 + step, n1)
            P.emit("gpsimd",
                   lambda e, o=buf[:].rearrange("p (n h) -> p n h", n=NCH)[:, s0:s1, :],
                   i=d_w1.rearrange("(n p) h -> p n h", p=128)[:, s0:s1,
                                                              q * 512:(q + 1) * 512]:
                   e.dma_start(out=o, in_=i),
                   waits if s0 == n0 else [], (f"w1q{q}", 16), f"w1l{q}.{s0}")
        w1q_ready[q] = P.cnt[f"w1q{q}"]

    def load_w1q(q, waits):
        load_w1q_seg(q, 0, NCH, waits)

    load_w1q_seg(0, 32, 60, [])

    t_L1 = [0] * NCH
    s_gelu = [0] * NCH
    for nch in range(NCH):
        p = nch % 2
        w = [("s", s_sq_last)] if nch < 2 else [("t", t_L1[nch - 2])]
        P.emit("gpsimd", lambda e, o=w0b[p][:].rearrange("p (c m) -> p c m", c=T),
               i=d_w0[nch]: e.dma_start(out=o, in_=i),
               w, (f"w0_{p}", 16), f"w0l{nch}")
        w0w = (f"w0_{p}", P.cnt[f"w0_{p}"])
        first = [w0w, ("s", vE_last), pfree[p]]
        for c in range(C4):
            for ec in range(T):
                stop = (ec == T - 1)
                inc = ("t", 1) if (stop and c == C4 - 1) else None
                P.emit("tensor", MM(None, ps[p][:, c * 512:(c + 1) * 512],
                                    w0b[p][:, ec * 128:(ec + 1) * 128],
                                    xnt1[:, ec * 2048 + c * 512: ec * 2048 + (c + 1) * 512],
                                    ec == 0, stop),
                       first if (ec == 0 and c == 0) else [], inc, f"L1_{nch}")
        tg = P.cnt["t"]
        t_L1[nch] = tg
        pready[p] = tg
        w = [("t", tg), ("b0", 16)]
        if debug:
            w.append(("dbgx", 16))
        if nch >= 2:
            w.append((f"dtw{p}", 16 * (nch // 2)))
        cs = P.emit("scalar", lambda e, o=dt_st[p][:], i=ps[p][:], b=b0_sb[:, nch:nch + 1]:
                    e.activation(o, i, AF.Gelu, bias=b),
                    w, ("s", 1), f"gelu{nch}")
        s_gelu[nch] = cs
        pfree[p] = ("s", cs)
        P.emit("scalar", lambda e, o=d_dt[:, :, nch, :].rearrange("sp p s -> p sp s"),
               i=dt_st[p][:].rearrange("p (sp s) -> p sp s", sp=SPP):
               e.dma_start(out=o, in_=i), [("s", cs)], (f"dtw{p}", 16), f"dtsp{nch}")
        if nch in (2, 4, 6, 8):
            i0 = 8 * (nch // 2 - 1)
            load_w1q_seg(0, i0, i0 + 8,
                         [("s", s_sq_last), ("v", v_ln_last)] if nch == 2 else [])
    t_L1_last = t_L1[NCH - 1]
    w1q0_partial = P.cnt["w1q0"]
    dtw_all = [("dtw0", P.cnt["dtw0"]), ("dtw1", P.cnt["dtw1"])]
    if debug:
        P.emit("sync", lambda e, o=dbg["xn2_dbg"], i=xn_sb[:]: e.dma_start(out=o, in_=i),
               [("t", t_L1_last)], ("dbg2", 16), "dbg_xn2")
        P.emit("sync", lambda e, o=dbg["xnt1_dbg"], i=xnt1[:]: e.dma_start(out=o, in_=i),
               [("t", t_L1_last)], ("dbg2", 16), "dbg_xnt1")

    dtr_cnt0 = [0, 0]
    dtr_pre = {}
    for sp in range(2):
        p = sp % 2
        w = dtw_all + [("t", t_L1_last), ("xtw", xtw_all)]
        for h in range(2):
            sem = f"dtrh{sp}" if h == 0 else f"dtr{p}"
            if h == 1:
                dtr_cnt0[p] += 16
            P.emit("sync",
                   lambda e, o=dts[p][:].rearrange("p (n s) -> p n s", n=NCH)[:, h * 32:(h + 1) * 32, :],
                   i=d_dt[sp][:, h * 32:(h + 1) * 32, :]:
                   e.dma_start(out=o, in_=i), w if h == 0 else [],
                   (sem, 16), f"dtr0.{sp}.{h}")
        dtr_pre[sp] = ((f"dtrh{sp}", 16), (f"dtr{p}", dtr_cnt0[p]))

    v_G = [0] * T
    gw_after = [0] * T
    mir_m = [0]
    n_mirrors = sum(T - 4 * (et // 4 + 1) for et in range(T) if 4 * (et // 4 + 1) < T)

    def emit_mirror(et, yt):
        m = mir_m[0]
        mir_m[0] += 1
        b = m % 2
        w = [(f"gw{yt % 2}", gw_after[yt])]
        if m >= 2:
            w.append(("mirW", 16 * (m - 1)))
        P.emit("scalar", lambda e, o=mst[b][:],
               i=d_G[et // 2, :, yt, (et % 2) * 128:(et % 2) * 128 + 128]:
               e.dma_start_transpose(out=o, in_=i), w, ("mirR", 16), f"mirR{m}")
        P.emit("scalar", lambda e,
               o=d_G[yt // 2, :, et, (yt % 2) * 128:(yt % 2) * 128 + 128],
               i=mst[b][:]:
               e.dma_start(out=o, in_=i), [("mirR", 16 * (m + 1))], ("mirW", 16), f"mirW{m}")

    late_mirrors = []
    for et in range(T):
        p = et % 2
        c_hi = et // 4 + 1
        first = [("v", v_ln_last), pfree[p]]
        for c in range(c_hi):
            for sc in range(T):
                stop = (sc == T - 1)
                inc = ("t", 1) if (stop and c == c_hi - 1) else None
                P.emit("tensor", MM(None, ps[p][:, c * 512:(c + 1) * 512],
                                    xn_sb[:, sc * 2048 + et * 128: sc * 2048 + (et + 1) * 128],
                                    xn_sb[:, sc * 2048 + c * 512: sc * 2048 + (c + 1) * 512],
                                    sc == 0, stop),
                       first if (sc == 0 and c == 0) else [], inc, f"G{et}")
        tg = P.cnt["t"]
        pready[p] = tg
        w = [("t", tg)]
        if et < 2:
            w += dtw_all
        if et >= 2:
            w.append((f"gw{p}", gw_after[et - 2]))
        cv = P.emit("vector", lambda e, o=Gst[p][:, :c_hi * 512], i=ps[p][:, :c_hi * 512]:
                    e.tensor_copy(o, i), w, ("v", 1), f"Gev{et}")
        v_G[et] = cv
        pfree[p] = ("v", cv)
        P.emit("sync", lambda e,
               o=d_G[:2 * c_hi, :, et, :].rearrange("ep p j -> p ep j"),
               i=Gst[p][:, :c_hi * 512].rearrange("p (ep j) -> p ep j", ep=2 * c_hi):
               e.dma_start(out=o, in_=i), [("v", cv)], (f"gw{p}", 16), f"Gsp{et}")
        gw_after[et] = P.cnt[f"gw{p}"]
        if et >= 4:
            tgts = [et2 for et2 in range(4 * (et // 4)) if et >= 4 * (et2 // 4 + 1)]
            if et < 8:
                for et2 in tgts:
                    emit_mirror(et2, et)
            else:
                late_mirrors.extend((et2, et) for et2 in tgts)
    t_G_last = pready[(T - 1) % 2]
    gw_all = [("gw0", P.cnt["gw0"]), ("gw1", P.cnt["gw1"])]
    mir_all = [("mirW", 16 * n_mirrors)]


    load_w1q_seg(0, 60, 64, gw_all)
    P.emit("gpsimd", lambda e, o=b1_sb[:], i=d_b1: e.dma_start(out=o, in_=i),
           [("t", t_L1_last)], ("b1", 16), "b1load")
    load_w1q_seg(1, 0, NCH, [("t", t_G_last)])

    t_L2 = {}
    v_bias_last = 0
    ggw_cnt = [0, 0]
    dtr_cnt = [0, 0]
    s_silu = {}
    for q in range(HQ):
        qb = w1q[0] if q % 2 == 0 else w1q[1]
        for sp in range(SPP):
            p = sp % 2
            split = (q == 0 and sp < 2)
            if split:
                dtr_cnt[p] = dtr_cnt0[p]
                dw_half, dw = dtr_pre[sp]
            else:
                prev = (q, sp - 2) if sp >= 2 else (q - 1, SPP - 2 + sp)
                w = [("t", t_L2[prev])]
                dtr_cnt[p] += 16
                P.emit("sync", lambda e, o=dts[p][:].rearrange("p (n s) -> p n s", n=NCH),
                       i=d_dt[sp]:
                       e.dma_start(out=o, in_=i), w, (f"dtr{p}", 16), f"dtr{q}.{sp}")
                dw_half = dw = (f"dtr{p}", dtr_cnt[p])
            qrdy = w1q0_partial if q == 0 else w1q_ready[q]
            first = [dw_half, (f"w1q{q}", qrdy), pfree[p]]
            for j in range(2):
                for n in range(NCH):
                    stop = (n == NCH - 1)
                    inc = ("t", 1) if (stop and j == 1) else None
                    ww = []
                    if n == 0 and j == 0:
                        ww = first
                    elif split and j == 0 and n == 32:
                        ww = [dw, dw_half]
                    if q == 0 and sp < 2 and j == 0 and n == 60:
                        ww = ww + [("w1q0", w1q_ready[0])]
                    P.emit("tensor", MM(None, ps[p][:, j * 512:(j + 1) * 512],
                                        dts[p][:, n * 256 + j * 128: n * 256 + (j + 1) * 128],
                                        qb[:, n * 512:(n + 1) * 512],
                                        n == 0, stop),
                           ww, inc, f"L2_{q}.{sp}")
            tg = P.cnt["t"]
            t_L2[(q, sp)] = tg
            pready[p] = tg
            last_v = 0
            for j in range(2):
                w = [("t", tg), ("b1", 16)]
                if q == 0 and sp == 0:
                    w = w + gw_all
                key = (q, sp - 1) if sp >= 1 else (q - 1, SPP - 1)
                if key in s_silu:
                    w.append(("s", s_silu[key][j]))
                cv = P.emit("vector", lambda e, o=e_st[j][:],
                            i=ps[p][:, j * 512:(j + 1) * 512], b=b1_sb[:, q * 512:(q + 1) * 512]:
                            e.tensor_tensor(o, i, b, ALU.add),
                            w, ("v", 1), f"bias{q}.{sp}.{j}")
                last_v = cv
                w2 = [("v", cv)]
                if ggw_cnt[j] > 0:
                    w2.append((f"ggw{j}", ggw_cnt[j]))
                cs = P.emit("scalar", lambda e, o=gst[j][:], i=e_st[j][:]:
                            e.activation(o, i, AF.Silu), w2, ("s", 1), f"silu{q}.{sp}.{j}")
                s_silu.setdefault((q, sp), {})[j] = cs
                ggw_cnt[j] += 16
                P.emit("scalar", lambda e, o=d_g[sp * 2 + j, :, q * 512:(q + 1) * 512],
                       i=gst[j][:]: e.dma_start(out=o, in_=i),
                       [("s", cs)], (f"ggw{j}", 16), f"gsp{q}.{sp}.{j}")
            pfree[p] = ("v", last_v)
            v_bias_last = last_v
            for _ in range(3):
                if late_mirrors:
                    emit_mirror(*late_mirrors.pop(0))
        if q + 2 < HQ:
            load_w1q(q + 2, [("t", t_L2[(q, SPP - 1)])])
    while late_mirrors:
        emit_mirror(*late_mirrors.pop(0))
    t_L2_last = t_L2[(HQ - 1, SPP - 1)]
    ggw_all = [("ggw0", P.cnt["ggw0"]), ("ggw1", P.cnt["ggw1"])]
    s_silu_last = s_silu[(HQ - 1, SPP - 1)][1]

    for c4 in range(4):
        P.emit("gpsimd", lambda e,
               o=wk_sb[:].rearrange("p (c f) -> p c f", c=T)[:, c4 * 4:(c4 + 1) * 4, :],
               i=d_wk.rearrange("(c p) f -> p c f", p=128)[:, c4 * 4:(c4 + 1) * 4, :]:
               e.dma_start(out=o, in_=i),
               [("t", t_L2[(2, SPP - 1)])] if c4 == 0 else [], ("wk", 16), f"wkl{c4}")
    wk_all = P.cnt["wk"]

    v_T1 = [0] * T
    t_T1_group = {}
    gr_cnt = [0, 0]
    for et in range(T):
        ep, j = et // 2, et % 2
        p2 = ep % 2
        if j == 0:
            if ep < 2:
                w = [("t", t_L2_last)] + gw_all + mir_all
            else:
                w = [("t", t_T1_group[2 * (ep - 2) + 1])]
            gr_cnt[p2] += 16
            P.emit("sync", lambda e, o=gpr[p2][:], i=d_G[ep]:
                   e.dma_start(out=o, in_=i.rearrange("p t j -> p (t j)")),
                   w, (f"gr{p2}", 16), f"gpr{ep}")
        grw = (f"gr{p2}", gr_cnt[p2])
        p = et % 2
        first = [grw, ("wk", wk_all), pfree[p]]
        for c in range(C4):
            for e2 in range(T):
                stop = (e2 == T - 1)
                inc = ("t", 1) if (stop and c == C4 - 1) else None
                P.emit("tensor", MM(None, ps[p][:, c * 512:(c + 1) * 512],
                                    gpr[p2][:, e2 * 256 + j * 128: e2 * 256 + (j + 1) * 128],
                                    wk_sb[:, e2 * 2048 + c * 512: e2 * 2048 + (c + 1) * 512],
                                    e2 == 0, stop),
                       first if (e2 == 0 and c == 0) else [], inc, f"T1_{et}")
        tg = P.cnt["t"]
        pready[p] = tg
        t_T1_group[et] = tg
        w = [("t", tg)]
        if et < 2:
            w += dtw_all
        cv = P.emit("vector", lambda e, o=t1_sb[:, et * 2048:(et + 1) * 2048], i=ps[p][:]:
                    e.tensor_copy(o, i), w, ("v", 1), f"T1ev{et}")
        v_T1[et] = cv
        pfree[p] = ("v", cv)
    t_T1_last = t_T1_group[T - 1]
    v_T1_last = v_T1[T - 1]

    for e_ in range(T):
        P.emit("sync", lambda e, o=xnt2[e_][:], i=d_xnt[e_]:
               e.dma_start(out=o, in_=i),
               [("t", t_T1_last)] if e_ == 0 else [], ("xnt2", 16), f"xnt2_{e_}")
    xnt2_all = P.cnt["xnt2"]

    ws_cnt = [0, 0, 0]
    ws_hcnt = [0, 0, 0]
    ws_user = {}

    def slice_load(gen, queue, fn, data_waits, tag):
        b = gen % 3
        kind = "wsg" if queue == "gpsimd" else "wsh"
        w = list(data_waits)
        if gen >= 3:
            w.append(("t", ws_user[gen - 3]))
        if gen < 3:
            if b < 2:
                w.append(("v", v_bias_last))
            else:
                w.append(("s", s_silu_last))
                w.extend(ggw_all)
        ws_cnt[b] = ws_cnt[b] + 16 if kind == "wsg" else ws_cnt[b]
        if kind == "wsg":
            P.emit(queue, fn, w, (f"wsg{b}", 16), tag)
            return (f"wsg{b}", ws_cnt[b])
        ws_hcnt[b] += 16
        P.emit(queue, fn, w, (f"wsh{b}", 16), tag)
        return (f"wsh{b}", ws_hcnt[b])

    s_exp = [0] * T
    v_A = [0] * T
    aw_cnt = [0, 0]
    t_scores_group = {}
    for et in range(T):
        p = et % 2
        b = et % 3
        wqw = slice_load(et, "gpsimd",
                         lambda e, o=wsb[b][:].rearrange("p (c m) -> p c m", c=T),
                         i=d_wq[et]: e.dma_start(out=o, in_=i), [], f"wql{et}")
        first = [wqw, ("v", v_T1_last), pfree[p]]
        for c in range(C4):
            for e1 in range(T):
                stop = (e1 == T - 1)
                inc = ("t", 1) if (stop and c == C4 - 1) else None
                P.emit("tensor", MM(None, ps[p][:, c * 512:(c + 1) * 512],
                                    wsb[b][:, e1 * 128:(e1 + 1) * 128],
                                    t1_sb[:, e1 * 2048 + c * 512: e1 * 2048 + (c + 1) * 512],
                                    e1 == 0, stop),
                       first if (e1 == 0 and c == 0) else [], inc, f"SC{et}")
        tg = P.cnt["t"]
        t_scores_group[et] = tg
        ws_user[et] = tg
        pready[p] = tg
        c1 = P.emit("vector", lambda e, o=smx[:], i=ps[p][:]:
                    e.reduce_max(o, i, axis=AX.X), [("t", tg)], ("v", 1), f"smax{et}")
        c2 = P.emit("vector", lambda e, o=snmx[:], i=smx[:]:
                    e.tensor_scalar(o, i, -0.125, None, op0=ALU.mult),
                    [("v", c1)], ("v", 1), f"snmx{et}")
        w = [("v", c2)]
        if et >= 1:
            w.append(("v", v_A[et - 1]))
        c3 = P.emit("scalar", lambda e, o=expb[:], i=ps[p][:], b=snmx[:], a=srsm[:]:
                    e.activation(o, i, AF.Exp, bias=b, scale=0.125, accum_out=a),
                    w, ("s", 1), f"sexp{et}")
        s_exp[et] = c3
        pfree[p] = ("s", c3)
        c4 = P.emit("vector", lambda e, o=srrc[:], i=srsm[:]:
                    e.reciprocal(o, i), [("s", c3)], ("v", 1), f"srrc{et}")
        w = [("v", c4)]
        if et >= 2:
            w.append((f"aw{p}", 16 * (et // 2)))
        c5 = P.emit("vector", lambda e, o=Ast[p][:], i=expb[:], r=srrc[:]:
                    e.tensor_scalar(o, i, r, None, op0=ALU.mult),
                    w, ("v", 1), f"sA{et}")
        v_A[et] = c5
        aw_cnt[p] += 16
        P.emit("scalar", lambda e, o=d_A[et], i=Ast[p][:]:
               e.dma_start(out=o, in_=i), [("v", c5)], (f"aw{p}", 16), f"Asp{et}")
    t_scores_last = t_scores_group[T - 1]
    aw_all = [("aw0", P.cnt["aw0"]), ("aw1", P.cnt["aw1"])]

    if debug:
        P.emit("sync", lambda e, o=dbg["t1_dbg"], i=t1_sb[:]: e.dma_start(out=o, in_=i),
               [("v", v_T1_last)], ("dbgw", 16), "dbg_t1")

    v_VT = [0] * T
    t_VT_group = {}
    for ft in range(T):
        p = ft % 2
        b = (16 + ft) % 3
        wvw = slice_load(16 + ft, "gpsimd",
                         lambda e, o=wsb[b][:].rearrange("p (c m) -> p c m", c=T),
                         i=d_wv[ft]: e.dma_start(out=o, in_=i), [], f"wvl{ft}")
        if ft in (3, 5, 7, 9):
            c4 = (ft - 3) // 2
            w_ = [("t", t_scores_last)] + ([("dbgw", 16)] if debug else [])
            P.emit("gpsimd", lambda e,
                   o=wo_sb[:].rearrange("p (c f) -> p c f", c=T)[:, c4 * 4:(c4 + 1) * 4, :],
                   i=d_wo.rearrange("(c p) f -> p c f", p=128)[:, c4 * 4:(c4 + 1) * 4, :]:
                   e.dma_start(out=o, in_=i),
                   w_ if c4 == 0 else [], ("wo", 16), f"wol{c4}")
        first = [wvw, ("xnt2", xnt2_all), pfree[p]]
        for c in range(C4):
            for ec in range(T):
                stop = (ec == T - 1)
                inc = ("t", 1) if (stop and c == C4 - 1) else None
                P.emit("tensor", MM(None, ps[p][:, c * 512:(c + 1) * 512],
                                    wsb[b][:, ec * 128:(ec + 1) * 128],
                                    xnt2[ec][:, c * 512:(c + 1) * 512],
                                    ec == 0, stop),
                       first if (ec == 0 and c == 0) else [], inc, f"VT{ft}")
        tg = P.cnt["t"]
        t_VT_group[ft] = tg
        ws_user[16 + ft] = tg
        pready[p] = tg
        w = [("t", tg)]
        if ft == 0:
            w.append(("v", v_A[T - 1]))
        if ft == 2:
            w.append(("aw0", 16 * (T // 2)))
        if ft == 3:
            w.append(("aw1", 16 * (T // 2)))
        cv = P.emit("vector", lambda e, o=vt_sb[:, ft * 2048:(ft + 1) * 2048], i=ps[p][:]:
                    e.tensor_copy(o, i), w, ("v", 1), f"VTev{ft}")
        v_VT[ft] = cv
        pfree[p] = ("v", cv)
    t_VT_last = t_VT_group[T - 1]
    wo_all = P.cnt["wo"]

    v_B = [0] * T
    t_B_group = {}
    for ft in range(T):
        p = ft % 2
        b = (32 + ft) % 3
        arw = slice_load(32 + ft, "sync" if ft % 2 == 0 else "scalar",
                         lambda e, o=wsb[b][:].rearrange("p (t m) -> p t m", t=T),
                         i=d_A[:, :, ft * 128:(ft + 1) * 128].rearrange("t k m -> k t m"):
                         e.dma_start(out=o, in_=i), aw_all, f"Ard{ft}")
        first = [arw, ("wo", wo_all), pfree[p]]
        for c in range(C4):
            for ec in range(T):
                stop = (ec == T - 1)
                inc = ("t", 1) if (stop and c == C4 - 1) else None
                P.emit("tensor", MM(None, ps[p][:, c * 512:(c + 1) * 512],
                                    wsb[b][:, ec * 128:(ec + 1) * 128],
                                    wo_sb[:, ec * 2048 + c * 512: ec * 2048 + (c + 1) * 512],
                                    ec == 0, stop),
                       first if (ec == 0 and c == 0) else [], inc, f"B{ft}")
        tg = P.cnt["t"]
        t_B_group[ft] = tg
        ws_user[32 + ft] = tg
        pready[p] = tg
        cv = P.emit("vector", lambda e, o=b_sb[:, ft * 2048:(ft + 1) * 2048], i=ps[p][:]:
                    e.tensor_copy(o, i), [("t", tg)], ("v", 1), f"Bev{ft}")
        v_B[ft] = cv
        pfree[p] = ("v", cv)
    t_B_last = t_B_group[T - 1]
    v_B_last = v_B[T - 1]

    if debug:
        P.emit("sync", lambda e, o=dbg["vt_dbg"], i=vt_sb[:]: e.dma_start(out=o, in_=i),
               [("v", v_VT[T - 1])], ("dbgw", 16), "dbg_vt")
        P.emit("sync", lambda e, o=dbg["b_dbg"], i=b_sb[:]: e.dma_start(out=o, in_=i),
               [("v", v_B_last)], ("dbgw", 16), "dbg_b")

    v_cmb = [0] * T
    for st_ in range(T):
        p = st_ % 2
        w = [("t", t_B_last)] if st_ < 2 else [("v", v_cmb[st_ - 2])]
        P.emit("sync", lambda e, o=xb2[p][:], i=d_x[st_ * 128:(st_ + 1) * 128, :]:
               e.dma_start(out=o, in_=i), w, (f"x2_{p}", 16), f"x2l{st_}")
        wg = list(w)
        if st_ < 2:
            wg += ggw_all
        P.emit("sync", lambda e, o=gb[p][:], i=d_g[st_]:
               e.dma_start(out=o, in_=i), wg, (f"gl{p}", 16), f"gl{st_}")
        xw = (f"x2_{p}", P.cnt[f"x2_{p}"])
        gw_ = (f"gl{p}", P.cnt[f"gl{p}"])
        first = [("v", v_B_last), pfree[p]]
        for c in range(C4):
            for f in range(T):
                stop = (f == T - 1)
                inc = ("t", 1) if (stop and c == C4 - 1) else None
                P.emit("tensor", MM(None, ps[p][:, c * 512:(c + 1) * 512],
                                    vt_sb[:, f * 2048 + st_ * 128: f * 2048 + (st_ + 1) * 128],
                                    b_sb[:, f * 2048 + c * 512: f * 2048 + (c + 1) * 512],
                                    f == 0, stop),
                       first if (f == 0 and c == 0) else [], inc, f"SA{st_}")
        tg = P.cnt["t"]
        pready[p] = tg
        c1 = P.emit("vector", lambda e, o=tmpb[:], i=ps[p][:], g_=gb[p][:]:
                    e.tensor_tensor(o, i, g_, ALU.mult),
                    [("t", tg), gw_], ("v", 1), f"cmul{st_}")
        w = [("v", c1), xw]
        if st_ >= 2:
            w.append((f"outw{p}", 16 * (st_ // 2)))
        c2 = P.emit("vector", lambda e, o=ob[p][:], i=tmpb[:], x_=xb2[p][:]:
                    e.tensor_tensor(o, i, x_, ALU.add), w, ("v", 1), f"cadd{st_}")
        v_cmb[st_] = c2
        pfree[p] = ("v", c1)
        P.emit("sync", lambda e, o=d_out[st_ * 128:(st_ + 1) * 128, :], i=ob[p][:]:
               e.dma_start(out=o, in_=i), [("v", c2)], (f"outw{p}", 16), f"outw{st_}")
    P.emit("sync", lambda e: e.nop(),
           [("outw0", P.cnt["outw0"]), ("outw1", P.cnt["outw1"])], None, "final")

    P.verify()

    with ExitStack() as stx:
        sems = {n: stx.enter_context(nc.semaphore(n)) for n in sorted(P.cnt)}
        block = stx.enter_context(nc.Block())

        def replay(name):
            def f(eng):
                for fn, ws, inc, tag in P.q[name]:
                    for s, v in ws:
                        eng.wait_ge(sems[s], v)
                    ins = fn(eng)
                    if inc is not None:
                        ins.then_inc(sems[inc[0]], inc[1])
            return f

        block.tensor(replay("tensor"))
        block.vector(replay("vector"))
        block.scalar(replay("scalar"))
        block.gpsimd(replay("gpsimd"))
        block.sync(replay("sync"))
    return nc


def prep_inputs(L0_attn_x, L0_attn_Wq, L0_attn_Wk, L0_attn_Wv, L0_attn_Wo,
                L0_ffn_W0, L0_ffn_bias0, L0_ffn_W1, L0_ffn_bias1):
    x = np.asarray(L0_attn_x, dtype=np.float32)
    Wq = np.asarray(L0_attn_Wq, dtype=np.float32)
    Wk = np.asarray(L0_attn_Wk, dtype=np.float32)
    Wv = np.asarray(L0_attn_Wv, dtype=np.float32)
    Wo = np.asarray(L0_attn_Wo, dtype=np.float32)
    W0 = np.asarray(L0_ffn_W0, dtype=np.float32)
    b0 = np.asarray(L0_ffn_bias0, dtype=np.float32)
    W1 = np.asarray(L0_ffn_W1, dtype=np.float32)
    b1 = np.asarray(L0_ffn_bias1, dtype=np.float32)

    wq_blk = np.ascontiguousarray(Wq.reshape(T, 128, T, 128).transpose(2, 1, 0, 3))
    wv_blk = np.ascontiguousarray(Wv.reshape(T, 128, T, 128).transpose(2, 1, 0, 3))
    w0t_blk = np.ascontiguousarray(W0.T.reshape(T, 128, NCH, 128).transpose(2, 1, 0, 3))
    w1t = np.ascontiguousarray(W1.T)
    b0t = np.ascontiguousarray(b0.reshape(NCH, 128).T)
    b1r = np.ascontiguousarray(np.broadcast_to(b1, (128, 2048)))
    shared = {
        "wq_blk": wq_blk, "wk": np.ascontiguousarray(Wk), "wv_blk": wv_blk,
        "wo": np.ascontiguousarray(Wo), "w0t_blk": w0t_blk, "w1t": w1t,
        "bias0_t": b0t, "bias1_rep": b1r,
        "iden": np.eye(128, dtype=np.float32),
    }
    return [dict(shared, x=np.ascontiguousarray(x[c])) for c in range(8)]


def kernel(**inputs):
    in_maps = prep_inputs(**inputs)
    nc = build_nc(debug=False)
    res = run_bass_kernel_spmd(nc, in_maps, core_ids=list(range(8)))
    return np.stack([np.asarray(res.results[c]["out"], dtype=np.float32)
                     for c in range(8)], axis=0)
